# version 06 (final)
# baseline (speedup 1.0000x reference)
import math
import numpy as np
import ml_dtypes
import concourse.bass as bass
import concourse.mybir as mybir
from concourse.bass_utils import run_bass_kernel_spmd

F32 = mybir.dt.float32
BF16 = mybir.dt.bfloat16
AF = mybir.ActivationFunctionType
ALU = mybir.AluOpType
AX = mybir.AxisListType
bf16 = ml_dtypes.bfloat16

D = 4096
SEQ = 8192
NCORES = 8
TPC = 1024
NT = 8
NDC = D // 128
HD = 128
NH = 32
SCALE = 1.0 / math.sqrt(HD)
ROPE_THETA = 10000.0
LN_EPS = 1e-5
MASKNEG = -1.0e30

L0 = dict(name="L0", kv=8, rope=True, qkv_bias=False, ffn="moe")
L1 = dict(name="L1", kv=32, rope=False, qkv_bias=True, ffn="chan")

SAME_ENGINE_EDGES = True
DEBUG = False



def shard_rows(core):
    idx = []
    for t in range(NT):
        a = 8 * t + core
        idx.extend(range(a * 128, (a + 1) * 128))
    return np.asarray(idx)


def fm_shard(x2d, core):
    return np.ascontiguousarray(x2d[shard_rows(core), :].T)


def unshard_fm(shards):
    out = np.empty((SEQ, D), dtype=np.asarray(shards[0]).dtype)
    for c in range(NCORES):
        out[shard_rows(c), :] = np.asarray(shards[c]).T
    return out


def wlay(WT, o_tile=128):
    Dd, O = WT.shape
    assert Dd == D and O % o_tile == 0
    a = WT.reshape(NDC, 128, O // o_tile, o_tile).transpose(1, 2, 0, 3)
    return np.ascontiguousarray(a.reshape(128, (O // o_tile) * NDC * o_tile))


def rope_tables(core):
    pos = shard_rows(core).astype(np.float64)
    inv = 1.0 / ROPE_THETA ** (np.arange(0, HD, 2, dtype=np.float64) / HD)
    fr = np.outer(inv, pos)
    cos = np.cos(fr).astype(np.float32)
    sin = np.sin(fr).astype(np.float32)
    return (np.ascontiguousarray(np.concatenate([cos, cos], axis=0)),
            np.ascontiguousarray(np.concatenate([-sin, sin], axis=0)))


def mask_add(core):
    m = np.full((128, 1024), MASKNEG, dtype=np.float32)
    for j in range(NCORES):
        if j < core:
            m[:, j * 128:(j + 1) * 128] = 0.0
        elif j == core:
            blk = np.where(np.tril(np.ones((128, 128), bool)), 0.0, MASKNEG)
            m[:, j * 128:(j + 1) * 128] = blk
    return m


CONSTS = dict(
    ones_col=np.ones((128, 1), np.float32),
    epscol=np.full((128, 1), LN_EPS, np.float32),
    ones_row=np.ones((1, 128), np.float32),
    identf=np.eye(128, dtype=np.float32),
    identb=np.eye(128, dtype=np.float32).astype(bf16),
    swapf=np.roll(np.eye(128, dtype=np.float32), 64, axis=1),
    expander=(2.0 * np.repeat(np.eye(8, dtype=np.float32), 16, axis=1)).astype(bf16),
)



ENGINES = ("sync", "gpsimd", "tensor", "vector", "scalar")


class Chain:
    __slots__ = ("h", "n", "name", "eng")

    def __init__(self, h, name, eng):
        self.h = h
        self.n = 0
        self.name = name
        self.eng = eng


class Buf:
    __slots__ = ("ap", "wr", "rd", "name")

    def __init__(self, ap, name=""):
        self.ap = ap
        self.wr = None
        self.rd = []
        self.name = name

    def __getitem__(self, sl):
        return self.ap[sl]

    @property
    def t(self):
        return self.ap


class P:
    def __init__(self, nc):
        self.nc = nc
        self.ops = []
        self._cm = []
        self._memo = {}
        self.ch = {e: self.chain("ch_" + e, e) for e in ENGINES}
        self.nops = 0

    def chain(self, name, eng):
        cm = self.nc.semaphore(name)
        h = cm.__enter__()
        self._cm.append(cm)
        return Chain(h, name, eng)

    def _wait_list(self, eng, deps, kind):
        w = []
        for d in deps:
            if d is None:
                continue
            ch, v, pkind = d
            if v <= 0:
                continue
            if (not SAME_ENGINE_EDGES and ch.eng == eng
                    and pkind == "compute" and kind == "compute"):
                continue
            key = (eng, ch.name)
            if self._memo.get(key, -1) >= v:
                continue
            self._memo[key] = v
            w.append((ch.h, v))
        return w

    def rec(self, eng, emit, reads=(), writes=(), kind="compute", extra=(),
            inc_n=None, chain=None):
        ch = chain if chain is not None else self.ch[eng]
        deps = []
        for b in reads:
            deps.append(b.wr)
        for b in writes:
            deps.append(b.wr)
            deps.extend(b.rd)
        deps.extend(extra)
        w = self._wait_list(eng, deps, kind)
        n = inc_n if inc_n is not None else (16 if kind == "dma" else 1)
        ch.n += n
        tok = (ch, ch.n, kind)

        def fn(e, _w=w, _emit=emit, _h=ch.h, _n=n):
            for hh, v in _w:
                e.wait_ge(hh, v)
            inst = _emit(e)
            inst.then_inc(_h, _n)
        self.ops.append((eng, fn))
        self.nops += 1
        for b in reads:
            b.rd.append(tok)
        for b in writes:
            b.wr = tok
            b.rd = []
        return tok

    def dma(self, queue, dst, src, reads=(), writes=(), extra=()):
        return self.rec(queue, lambda e: e.dma_start(out=dst, in_=src),
                        reads=reads, writes=writes, kind="dma", extra=extra)

    def barrier_all(self):
        toks = [(c, c.n, "dma") for c in self.ch.values() if c.n > 0]
        for eng in ENGINES:
            w = self._wait_list(eng, toks, "compute")
            if not w:
                continue

            def fn(e, _w=w):
                for hh, v in _w:
                    e.wait_ge(hh, v)
            self.ops.append((eng, fn))

    def finish(self):
        with self.nc.Block() as block:
            for name in ENGINES:
                sel = [fn for (eng, fn) in self.ops if eng == name]
                if not sel:
                    continue

                def body(e, _sel=sel):
                    for fn in _sel:
                        fn(e)
                getattr(block, name)(body)
        for cm in self._cm:
            cm.__exit__(None, None, None)


class Alloc:

    def __init__(self, nc):
        self.nc = nc
        self._cm = []

    def sb(self, name, shape, dt):
        cm = self.nc.sbuf_tensor(name, shape, dt)
        t = cm.__enter__()
        self._cm.append(cm)
        return Buf(t.ap() if hasattr(t, "ap") else t, name)

    def psum(self, name, shape, dt):
        cm = self.nc.psum_tensor(name, shape, dt)
        t = cm.__enter__()
        self._cm.append(cm)
        return Buf(t.ap() if hasattr(t, "ap") else t, name)

    def close(self):
        for cm in reversed(self._cm):
            cm.__exit__(None, None, None)



def make_pool(al):
    return {
        "PM": [al.psum(f"pool_pm{i}", [128, 512], F32) for i in range(4)],
        "PT": [al.psum(f"pool_pt{i}", [128, 512], BF16) for i in range(2)],
        "PO": [al.psum(f"pool_po{i}", [128, 128], F32) for i in range(2)],
    }


def emit_consts(p, al, io):
    c = {}
    for nm, shape, dt in (("ones_col", (128, 1), F32), ("epscol", (128, 1), F32),
                          ("ones_row", (1, 128), F32), ("identf", (128, 128), F32),
                          ("identb", (128, 128), BF16), ("swapf", (128, 128), F32),
                          ("expander", (8, 128), BF16),
                          ("maskA", (128, 1024), F32)):
        b = al.sb("s_" + nm, list(shape), dt)
        p.dma("sync", b.ap[:, :], io[nm], writes=[b])
        c[nm] = b
    return c


def emit_layernorm(p, al, C, PS, SC, io_x, xn, x_dep=None):
    xa = SC["f"][0:2]
    sq = SC["f"][2:4]
    pm = PS["PM"]

    for c in range(NDC):
        b = c % 2
        p.dma("sync", xa[b][:, :], io_x[c * 128:(c + 1) * 128, :],
              writes=[xa[b]], reads=[x_dep] if x_dep is not None else ())
        p.rec("scalar", lambda e, b=b: e.activation(sq[b][:, :], xa[b][:, :],
                                                    AF.Square),
              reads=[xa[b]], writes=[sq[b]])
        first, last = c == 0, c == NDC - 1
        for tb in range(2):
            sl = slice(tb * 512, (tb + 1) * 512)
            p.rec("tensor", lambda e, b=b, sl=sl, tb=tb, first=first, last=last:
                  e.matmul(pm[tb][0:1, :], C["ones_col"][:, :], xa[b][:, sl],
                           start=first, stop=last),
                  reads=[xa[b], C["ones_col"]],
                  writes=[] if not last else [pm[tb]],
                  extra=[pm[tb].wr] if first else ())
            p.rec("tensor", lambda e, b=b, sl=sl, tb=tb, first=first, last=last:
                  e.matmul(pm[2 + tb][0:1, :], C["ones_col"][:, :], sq[b][:, sl],
                           start=first, stop=last),
                  reads=[sq[b], C["ones_col"]],
                  writes=[] if not last else [pm[2 + tb]],
                  extra=[pm[2 + tb].wr] if first else ())

    st = SC["stat"]
    ns = SC["nslab"]

    def stv(i):
        if i == 2:
            return ns[0:1, 0:TPC]
        return st[0:1, i * TPC:(i + 1) * TPC]

    def stv_h(i, tb):
        if i == 2:
            return ns[0:1, tb * 512:tb * 512 + 512]
        return st[0:1, i * TPC + tb * 512: i * TPC + tb * 512 + 512]

    for tb in range(2):
        p.rec("scalar", lambda e, tb=tb: e.activation(
            stv_h(0, tb), pm[tb][0:1, :], AF.Copy, scale=1.0 / D),
            reads=[pm[tb]], writes=[st])
        p.rec("scalar", lambda e, tb=tb: e.activation(
            stv_h(1, tb), pm[2 + tb][0:1, :], AF.Copy, scale=1.0 / D),
            reads=[pm[2 + tb]], writes=[st])
    p.rec("scalar", lambda e: e.activation(stv(2), stv(0), AF.Square),
          reads=[st], writes=[ns])
    p.rec("vector", lambda e: e.tensor_sub(stv(1), stv(1), stv(2)),
          reads=[st, ns], writes=[st])
    p.rec("scalar", lambda e: e.activation(stv(1), stv(1), AF.Sqrt,
                                           bias=C["epscol"][0:1, :]),
          reads=[st, C["epscol"]], writes=[st])
    p.rec("vector", lambda e: e.reciprocal(stv(1), stv(1)),
          reads=[st], writes=[st])
    p.rec("vector", lambda e: e.tensor_mul(stv(2), stv(0), stv(1)),
          reads=[st], writes=[ns])
    p.rec("vector", lambda e: e.tensor_scalar_mul(stv(2), stv(2), -1.0),
          reads=[ns], writes=[ns])

    abc = SC["abc"]
    bbc = SC["bbc"]
    tmp = SC["f"][2]
    for tb in range(2):
        sl = slice(tb * 512, (tb + 1) * 512)
        p.rec("tensor", lambda e, sl=sl, tb=tb: e.matmul(
            pm[tb][:, :], C["ones_row"][:, :], stv_h(1, tb),
            start=True, stop=True),
            reads=[st, C["ones_row"]], writes=[pm[tb]])
        p.rec("vector", lambda e, sl=sl, tb=tb: e.tensor_copy(abc[:, sl],
                                                              pm[tb][:, :]),
              reads=[pm[tb]], writes=[abc])
        p.rec("tensor", lambda e, sl=sl, tb=tb: e.matmul(
            pm[2 + tb][:, :], C["ones_row"][:, :], stv_h(2, tb),
            start=True, stop=True),
            reads=[ns, C["ones_row"]], writes=[pm[2 + tb]])
        p.rec("vector", lambda e, sl=sl, tb=tb: e.tensor_copy(bbc[:, sl],
                                                              pm[2 + tb][:, :]),
              reads=[pm[2 + tb]], writes=[bbc])

    for c in range(NDC):
        b = c % 2
        p.dma("sync", xa[b][:, :], io_x[c * 128:(c + 1) * 128, :],
              writes=[xa[b]], reads=[x_dep] if x_dep is not None else ())
        p.rec("vector", lambda e, b=b: e.tensor_mul(tmp[:, :], xa[b][:, :],
                                                    abc[:, :]),
              reads=[xa[b], abc], writes=[tmp])
        p.rec("vector", lambda e, c=c: e.tensor_add(
            xn[:, c * TPC:(c + 1) * TPC], tmp[:, :], bbc[:, :]),
            reads=[tmp, bbc], writes=[xn])
    return


def emit_projections(p, al, C, PS, SC, cfg, io, xn):
    KV = cfg["kv"]
    VOFF = KV * TPC
    wbuf = SC["w"]
    pm = PS["PM"][0:2]
    pn = PS["PM"][2:4]
    pt = PS["PT"]
    ktile = SC["f"][0]
    rtmp = SC["f"][1]
    rtile = SC["b"][0]
    vtr = SC["b"][1]
    ccin = io["ccin"]
    ccin_buf = io["ccin_buf"]
    if cfg["rope"]:
        cosT = io["cosT_sb"]
        sinS = io["sinS_sb"]
        p.dma("sync", cosT[:, :], io["cosT"], writes=[cosT])
        p.dma("sync", sinS[:, :], io["sinS"], writes=[sinS])
    if cfg["qkv_bias"]:
        qkvb = io["qkvb_sb"]
        p.dma("sync", qkvb[:, 0:3 * NDC], io["qkvb"], writes=[qkvb])

    state = {"w": 0}

    def head(wsrc, bias_col, do_rope):
        b = state["w"] % 2
        state["w"] += 1
        p.dma("sync", wbuf[b][:, :], wsrc, writes=[wbuf[b]])
        for tb in range(2):
            for c in range(NDC):
                p.rec("tensor", lambda e, b=b, c=c, tb=tb: e.matmul(
                    pm[tb][:, :], wbuf[b][:, c * 128:(c + 1) * 128],
                    xn[:, c * TPC + tb * 512: c * TPC + tb * 512 + 512],
                    start=(c == 0), stop=(c == NDC - 1)),
                    reads=[wbuf[b], xn],
                    writes=[pm[tb]] if c == NDC - 1 else [],
                    extra=([pm[tb].wr] + pm[tb].rd) if c == 0 else ())
        for tb in range(2):
            sl = slice(tb * 512, (tb + 1) * 512)
            if bias_col is None:
                p.rec("scalar", lambda e, tb=tb, sl=sl: e.activation(
                    ktile[:, sl], pm[tb][:, :], AF.Copy),
                    reads=[pm[tb]], writes=[ktile] if tb == 1 else [],
                    extra=(ktile.rd + [ktile.wr]) if tb == 0 else ())
            else:
                p.rec("vector", lambda e, tb=tb, sl=sl, bc=bias_col:
                      e.tensor_scalar_add(ktile[:, sl], pm[tb][:, :],
                                          qkvb[:, bc:bc + 1]),
                      reads=[pm[tb], qkvb], writes=[ktile] if tb == 1 else [],
                      extra=(ktile.rd + [ktile.wr]) if tb == 0 else ())
        if do_rope:
            for tb in range(2):
                sl = slice(tb * 512, (tb + 1) * 512)
                p.rec("tensor", lambda e, tb=tb, sl=sl: e.matmul(
                    pn[tb][:, :], C["swapf"][:, :], ktile[:, sl],
                    start=True, stop=True),
                    reads=[ktile, C["swapf"]], writes=[pn[tb]])
                p.rec("vector", lambda e, tb=tb, sl=sl: e.tensor_mul(
                    rtmp[:, sl], pn[tb][:, :], sinS[:, sl]),
                    reads=[pn[tb], sinS], writes=[rtmp] if tb == 1 else [],
                    extra=(rtmp.rd + [rtmp.wr]) if tb == 0 else ())
            p.rec("vector", lambda e: e.tensor_mul(ktile[:, :], ktile[:, :],
                                                   cosT[:, :]),
                  reads=[ktile, cosT], writes=[ktile])
            p.rec("vector", lambda e: e.tensor_add(rtile[:, :], ktile[:, :],
                                                   rtmp[:, :]),
                  reads=[ktile, rtmp], writes=[rtile])
        else:
            p.rec("vector", lambda e: e.tensor_copy(rtile[:, :], ktile[:, :]),
                  reads=[ktile], writes=[rtile])
        return rtile

    for h in range(KV):
        head(io["WkT"][:, h * (NDC * 128):(h + 1) * (NDC * 128)],
             (NDC + h) if cfg["qkv_bias"] else None, cfg["rope"])
        p.dma("scalar", ccin[:, h * TPC:(h + 1) * TPC], rtile[:, :],
              reads=[rtile], writes=[ccin_buf], extra=[ccin_buf.wr])

    for h in range(KV):
        head(io["WvT"][:, h * (NDC * 128):(h + 1) * (NDC * 128)],
             (2 * NDC + h) if cfg["qkv_bias"] else None, False)
        for grp in range(2):
            for j in range(4):
                tt = grp * 4 + j
                p.rec("tensor", lambda e, tt=tt, j=j, grp=grp: e.transpose(
                    pt[grp][:, j * 128:(j + 1) * 128],
                    rtile[:, tt * 128:(tt + 1) * 128], C["identb"][:, :]),
                    reads=[rtile, C["identb"]],
                    writes=[pt[grp]] if j == 3 else [],
                    extra=([pt[grp].wr] + pt[grp].rd) if j == 0 else ())
            p.rec("vector", lambda e, grp=grp: e.tensor_copy(
                vtr[:, grp * 512:(grp + 1) * 512], pt[grp][:, :]),
                reads=[pt[grp]], writes=[vtr] if grp == 1 else [],
                extra=(vtr.rd + [vtr.wr]) if grp == 0 else ())
        p.dma("scalar", ccin[:, VOFF + h * TPC: VOFF + (h + 1) * TPC],
              vtr[:, :], reads=[vtr], writes=[ccin_buf], extra=[ccin_buf.wr])


def emit_q_heads(p, al, C, PS, SC, cfg, io, xn):
    wbuf = SC["w"]
    pm = PS["PM"][0:2]
    pn = PS["PM"][2:4]
    ktile = SC["f"][0]
    rtmp = SC["f"][1]
    rtile = SC["b"][0]
    qT_buf = io["qT_buf"]
    if cfg["rope"]:
        cosT = io["cosT_sb"]
        sinS = io["sinS_sb"]
    if cfg["qkv_bias"]:
        qkvb = io["qkvb_sb"]

    for h in range(NH):
        b = h % 2
        p.dma("sync", wbuf[b][:, :],
              io["WqT"][:, h * (NDC * 128):(h + 1) * (NDC * 128)],
              writes=[wbuf[b]])
        for tb in range(2):
            for c in range(NDC):
                p.rec("tensor", lambda e, b=b, c=c, tb=tb: e.matmul(
                    pm[tb][:, :], wbuf[b][:, c * 128:(c + 1) * 128],
                    xn[:, c * TPC + tb * 512: c * TPC + tb * 512 + 512],
                    start=(c == 0), stop=(c == NDC - 1)),
                    reads=[wbuf[b], xn],
                    writes=[pm[tb]] if c == NDC - 1 else [],
                    extra=([pm[tb].wr] + pm[tb].rd) if c == 0 else ())
        for tb in range(2):
            sl = slice(tb * 512, (tb + 1) * 512)
            if cfg["qkv_bias"]:
                p.rec("vector", lambda e, tb=tb, sl=sl, bc=h:
                      e.tensor_scalar_add(ktile[:, sl], pm[tb][:, :],
                                          qkvb[:, bc:bc + 1]),
                      reads=[pm[tb], qkvb], writes=[ktile] if tb == 1 else [],
                      extra=(ktile.rd + [ktile.wr]) if tb == 0 else ())
            else:
                p.rec("scalar", lambda e, tb=tb, sl=sl: e.activation(
                    ktile[:, sl], pm[tb][:, :], AF.Copy),
                    reads=[pm[tb]], writes=[ktile] if tb == 1 else [],
                    extra=(ktile.rd + [ktile.wr]) if tb == 0 else ())
        if cfg["rope"]:
            for tb in range(2):
                sl = slice(tb * 512, (tb + 1) * 512)
                p.rec("tensor", lambda e, tb=tb, sl=sl: e.matmul(
                    pn[tb][:, :], C["swapf"][:, :], ktile[:, sl],
                    start=True, stop=True),
                    reads=[ktile, C["swapf"]], writes=[pn[tb]])
                p.rec("vector", lambda e, tb=tb, sl=sl: e.tensor_mul(
                    rtmp[:, sl], pn[tb][:, :], sinS[:, sl]),
                    reads=[pn[tb], sinS], writes=[rtmp] if tb == 1 else [],
                    extra=(rtmp.rd + [rtmp.wr]) if tb == 0 else ())
            p.rec("vector", lambda e: e.tensor_mul(ktile[:, :], ktile[:, :],
                                                   cosT[:, :]),
                  reads=[ktile, cosT], writes=[ktile])
            p.rec("vector", lambda e: e.tensor_add(rtile[:, :], ktile[:, :],
                                                   rtmp[:, :]),
                  reads=[ktile, rtmp], writes=[rtile])
        else:
            p.rec("vector", lambda e: e.tensor_copy(rtile[:, :], ktile[:, :]),
                  reads=[ktile], writes=[rtile])
        p.dma("scalar", io["qT"][:, h * TPC:(h + 1) * TPC], rtile[:, :],
              reads=[rtile], writes=[qT_buf], extra=[qT_buf.wr])


def emit_router(p, al, C, PS, SC, io_wT, wr, xn, topk):
    p.dma("sync", wr.ap[:, :], io_wT, writes=[wr])
    pm = PS["PM"][0:2]
    ptt = PS["PM"][2]
    sm = SC["small"]
    LT = slice(0, 8)
    PT_ = slice(8, 16)
    M1, NM1, SU, RS = 16, 17, 18, 19
    M2 = 20
    MS = slice(24, 32)
    lgT = SC["f"][3]
    cmbT = SC["cmbT"]
    for tb in range(2):
        sl = slice(tb * 512, (tb + 1) * 512)
        for c in range(NDC):
            p.rec("tensor", lambda e, c=c, tb=tb: e.matmul(
                pm[tb][0:8, :], wr[:, c * 8:(c + 1) * 8],
                xn[:, c * TPC + tb * 512: c * TPC + tb * 512 + 512],
                start=(c == 0), stop=(c == NDC - 1)),
                reads=[wr, xn], writes=[pm[tb]] if c == NDC - 1 else [],
                extra=([pm[tb].wr] + pm[tb].rd) if c == 0 else ())
        p.rec("vector", lambda e, tb=tb, sl=sl: e.tensor_copy(lgT[0:8, sl],
                                                              pm[tb][0:8, :]),
              reads=[pm[tb]], writes=[lgT] if tb == 1 else [],
              extra=(lgT.rd + [lgT.wr]) if tb == 0 else ())
    for tt in range(NT):
        p.rec("tensor", lambda e, tt=tt: e.transpose(
            ptt[:, 0:8], lgT[0:8, tt * 128:(tt + 1) * 128],
            C["identf"][0:8, 0:8]),
            reads=[lgT, C["identf"]], writes=[ptt])
        p.rec("vector", lambda e: e.tensor_copy(sm[:, LT], ptt[:, 0:8]),
              reads=[ptt], writes=[sm])
        p.rec("vector", lambda e: e.reduce_max(sm[:, M1:NM1], sm[:, LT], axis=AX.X),
              reads=[sm], writes=[sm])
        p.rec("vector", lambda e: e.tensor_scalar_mul(sm[:, NM1:SU], sm[:, M1:NM1],
                                                      -1.0),
              reads=[sm], writes=[sm])
        p.rec("scalar", lambda e: e.activation(sm[:, PT_], sm[:, LT], AF.Exp,
                                               bias=sm[:, NM1:SU],
                                               accum_out=sm[:, SU:RS]),
              reads=[sm], writes=[sm])
        p.rec("vector", lambda e: e.reciprocal(sm[:, RS:RS + 1], sm[:, SU:RS]),
              reads=[sm], writes=[sm])
        p.rec("vector", lambda e: e.tensor_scalar_mul(sm[:, PT_], sm[:, PT_],
                                                      sm[:, RS:RS + 1]),
              reads=[sm], writes=[sm])
        if topk:
            p.rec("vector", lambda e: e.reduce_max(sm[:, M1:NM1], sm[:, PT_],
                                                   axis=AX.X),
                  reads=[sm], writes=[sm])
            p.rec("vector", lambda e: e.tensor_single_scalar(
                sm[:, MS], sm[:, PT_], sm[:, M1:NM1], ALU.is_lt),
                reads=[sm], writes=[sm])
            p.rec("vector", lambda e: e.tensor_mul(sm[:, MS], sm[:, PT_],
                                                   sm[:, MS]),
                  reads=[sm], writes=[sm])
            p.rec("vector", lambda e: e.reduce_max(sm[:, M2:M2 + 1], sm[:, MS],
                                                   axis=AX.X),
                  reads=[sm], writes=[sm])
            p.rec("vector", lambda e: e.tensor_single_scalar(
                sm[:, MS], sm[:, PT_], sm[:, M2:M2 + 1], ALU.is_ge),
                reads=[sm], writes=[sm])
            p.rec("vector", lambda e: e.tensor_mul(sm[:, PT_], sm[:, PT_],
                                                   sm[:, MS]),
                  reads=[sm], writes=[sm])
            p.rec("vector", lambda e: e.tensor_add(sm[:, SU:RS], sm[:, M1:NM1],
                                                   sm[:, M2:M2 + 1]),
                  reads=[sm], writes=[sm])
            p.rec("vector", lambda e: e.reciprocal(sm[:, RS:RS + 1], sm[:, SU:RS]),
                  reads=[sm], writes=[sm])
            p.rec("vector", lambda e: e.tensor_scalar_mul(sm[:, PT_], sm[:, PT_],
                                                          sm[:, RS:RS + 1]),
                  reads=[sm], writes=[sm])
        p.rec("tensor", lambda e: e.transpose(ptt[0:8, 0:128], sm[:, PT_],
                                              C["identf"][:, :]),
              reads=[sm, C["identf"]], writes=[ptt])
        p.rec("vector", lambda e, tt=tt: e.tensor_copy(
            cmbT[0:8, tt * 128:(tt + 1) * 128], ptt[0:8, 0:128]),
            reads=[ptt], writes=[cmbT] if tt == NT - 1 else [],
            extra=(cmbT.rd + [cmbT.wr]) if tt == 0 else ())
    return cmbT


def emit_ffn_moe(p, al, C, PS, SC, AT, io, xn):
    cmbT = emit_router(p, al, C, PS, SC, io["WrT"], AT["wr0"], xn, topk=True)
    atb = AT["atb"]
    btb = AT["btb"]
    p.dma("sync", atb.ap[:, :], io["AT"], writes=[atb])
    p.dma("sync", btb.ap[:, :], io["BT"], writes=[btb])
    bb = AT["bb"]
    p.dma("sync", bb.ap[:, :], io["bb"], writes=[bb])
    pm = PS["PM"][0:2]
    pc = PS["PM"][2:4]
    hT = SC["b"][0]
    cb = SC["f"][0]
    hw = SC["b"][1]
    for tb in range(2):
        sl = slice(tb * 512, (tb + 1) * 512)
        for c in range(NDC):
            p.rec("tensor", lambda e, c=c, tb=tb: e.matmul(
                pm[tb][:, :], atb[:, c * 128:(c + 1) * 128],
                xn[:, c * TPC + tb * 512: c * TPC + tb * 512 + 512],
                start=(c == 0), stop=(c == NDC - 1)),
                reads=[atb, xn], writes=[pm[tb]] if c == NDC - 1 else [],
                extra=([pm[tb].wr] + pm[tb].rd) if c == 0 else ())
        p.rec("scalar", lambda e, tb=tb, sl=sl: e.activation(
            hT[:, sl], pm[tb][:, :], AF.Copy),
            reads=[pm[tb]], writes=[hT] if tb == 1 else [],
            extra=(hT.rd + [hT.wr]) if tb == 0 else ())
        p.rec("tensor", lambda e, tb=tb: e.matmul(
            pc[tb][:, :], C["expander"][:, :], cmbT[:, tb * 512:(tb + 1) * 512],
            start=True, stop=True),
            reads=[C["expander"], cmbT], writes=[pc[tb]])
        p.rec("vector", lambda e, tb=tb, sl=sl: e.tensor_copy(cb[:, sl],
                                                              pc[tb][:, :]),
              reads=[pc[tb]], writes=[cb] if tb == 1 else [],
              extra=(cb.rd + [cb.wr]) if tb == 0 else ())
    p.rec("vector", lambda e: e.tensor_mul(hw[:, :], hT[:, :], cb[:, :]),
          reads=[hT, cb], writes=[hw])

    wbuf = SC["w"]
    ffb = SC["b"][0]
    for oc in range(NDC):
        b = oc % 2
        p.dma("sync", wbuf[b][:, :],
              io["WbT"][:, oc * (NDC * 128):(oc + 1) * (NDC * 128)],
              writes=[wbuf[b]])
        for tb in range(2):
            for c in range(NDC):
                p.rec("tensor", lambda e, b=b, c=c, tb=tb: e.matmul(
                    pm[tb][:, :], wbuf[b][:, c * 128:(c + 1) * 128],
                    xn[:, c * TPC + tb * 512: c * TPC + tb * 512 + 512],
                    start=(c == 0), stop=False),
                    reads=[wbuf[b], xn], writes=[],
                    extra=([pm[tb].wr] + pm[tb].rd) if c == 0 else ())
            p.rec("tensor", lambda e, tb=tb, oc=oc: e.matmul(
                pm[tb][:, :], btb[:, oc * 128:(oc + 1) * 128],
                hw[:, tb * 512:(tb + 1) * 512], start=False, stop=True),
                reads=[btb, hw], writes=[pm[tb]])
        for tb in range(2):
            sl = slice(tb * 512, (tb + 1) * 512)
            p.rec("vector", lambda e, tb=tb, sl=sl, oc=oc: e.tensor_scalar_add(
                ffb[:, sl], pm[tb][:, :], bb[:, oc:oc + 1]),
                reads=[pm[tb], bb], writes=[ffb] if tb == 1 else [],
                extra=(ffb.rd + [ffb.wr]) if tb == 0 else ())
        p.dma("scalar", io["ffn_out"][oc * 128:(oc + 1) * 128, :], ffb[:, :],
              reads=[ffb], writes=[io["ffn_buf"]], extra=[io["ffn_buf"].wr])


def emit_ffn_chan(p, al, C, PS, SC, AT, io, xn):
    cmbT = emit_router(p, al, C, PS, SC, io["gateT"], AT["wr1"], xn, topk=False)
    ics = AT["atb"]
    ocs = AT["btb"]
    p.dma("sync", ics.ap[0:8, 0:D], io["ics"], writes=[ics])
    p.dma("sync", ocs.ap[0:8, 0:D], io["ocs"], writes=[ocs])
    pm = PS["PM"][0:2]
    po = PS["PM"][2:4]
    for c in range(NDC):
        for tb in range(2):
            sl = slice(c * TPC + tb * 512, c * TPC + tb * 512 + 512)
            p.rec("tensor", lambda e, c=c, tb=tb: e.matmul(
                pm[tb][:, :], ics.ap[0:8, c * 128:(c + 1) * 128],
                cmbT[:, tb * 512:(tb + 1) * 512], start=True, stop=True),
                reads=[ics, cmbT], writes=[pm[tb]])
            p.rec("vector", lambda e, sl=sl, tb=tb: e.tensor_mul(
                xn[:, sl], xn[:, sl], pm[tb][:, :]),
                reads=[xn, pm[tb]], writes=[xn])
    wbuf = SC["w"]
    wsg = wbuf
    osc = SC["f"][0]
    ffo = SC["b"][0]
    for oc in range(NDC):
        b = oc % 2
        p.dma("sync", wbuf[b][:, :],
              io["WsT"][:, oc * (NDC * 128):(oc + 1) * (NDC * 128)],
              writes=[wbuf[b]])
        p.rec("scalar", lambda e, b=b: e.sign(wsg[b][:, :], wbuf[b][:, :]),
              reads=[wbuf[b]], writes=[wbuf[b]])
        for tb in range(2):
            for c in range(NDC):
                p.rec("tensor", lambda e, b=b, c=c, tb=tb: e.matmul(
                    pm[tb][:, :], wsg[b][:, c * 128:(c + 1) * 128],
                    xn[:, c * TPC + tb * 512: c * TPC + tb * 512 + 512],
                    start=(c == 0), stop=(c == NDC - 1)),
                    reads=[wsg[b], xn], writes=[pm[tb]] if c == NDC - 1 else [],
                    extra=([pm[tb].wr] + pm[tb].rd) if c == 0 else ())
            p.rec("tensor", lambda e, tb=tb, oc=oc: e.matmul(
                po[tb][:, :], ocs.ap[0:8, oc * 128:(oc + 1) * 128],
                cmbT[:, tb * 512:(tb + 1) * 512], start=True, stop=True),
                reads=[ocs, cmbT], writes=[po[tb]])
        for tb in range(2):
            sl = slice(tb * 512, (tb + 1) * 512)
            p.rec("vector", lambda e, tb=tb, sl=sl: e.tensor_copy(
                osc[:, sl], po[tb][:, :]),
                reads=[po[tb]], writes=[osc] if tb == 1 else [],
                extra=(osc.rd + [osc.wr]) if tb == 0 else ())
            p.rec("vector", lambda e, tb=tb, sl=sl: e.tensor_mul(
                ffo[:, sl], pm[tb][:, :], osc[:, sl]),
                reads=[pm[tb], osc], writes=[ffo] if tb == 1 else [],
                extra=(ffo.rd + [ffo.wr]) if tb == 0 else ())
        p.dma("scalar", io["ffn_out"][oc * 128:(oc + 1) * 128, :], ffo[:, :],
              reads=[ffo], writes=[io["ffn_buf"]], extra=[io["ffn_buf"].wr])



def make_attn_bufs(al):
    return {
        "kbuf": al.sb("at_k", [128, SEQ], BF16),
        "vbuf": al.sb("at_v", [128, SEQ], BF16),
        "pstrip": al.sb("at_p", [128, SEQ], BF16),
        "ptb": [al.sb(f"at_ptb{i}", [128, 512], BF16) for i in range(2)],
        "qb": [al.sb(f"at_q{i}", [128, TPC], BF16) for i in range(2)],
        "lsl": al.sb("at_l", [128, 8], F32),
        "atb": al.sb("mo_at", [128, NDC * 128], BF16),
        "btb": al.sb("mo_bt", [128, D], BF16),
        "bb": al.sb("mo_bb", [128, NDC], F32),
        "wr0": al.sb("rt_w0", [128, NDC * 8], BF16),
        "wr1": al.sb("rt_w1", [128, NDC * 8], BF16),
    }


def emit_attention(p, al, C, PS, SC, AT, cfg, io, att, core):
    KV = cfg["kv"]
    GS = NH // KV
    VOFF = KV * TPC
    kbuf = AT["kbuf"]
    vbuf = AT["vbuf"]
    pstrip = AT["pstrip"]
    ptb = AT["ptb"]
    qb = AT["qb"]
    lsl = AT["lsl"]
    PM = PS["PM"]
    PT = PS["PT"]
    PO = PS["PO"]
    ccr = io["ccr"]
    cco_buf = io["cco_buf"]
    qT_buf = io["qT_buf"]

    for g in range(KV):
        p.dma("sync", kbuf[:, :], ccr[:, :, g * TPC:(g + 1) * TPC],
              writes=[kbuf], reads=[cco_buf])
        p.dma("scalar", vbuf[:, :],
              ccr[:, :, VOFF + g * TPC: VOFF + (g + 1) * TPC],
              writes=[vbuf], reads=[cco_buf])
        for hh in range(GS):
            h = g * GS + hh
            qp = h % 2
            p.dma("sync", qb[qp][:, :], io["qT"][:, h * TPC:(h + 1) * TPC],
                  writes=[qb[qp]], reads=[qT_buf])
            for t in range(NT):
                L = (t + 1) * 128
                pidx = 0
                for cp in range(8):
                    base = cp * TPC
                    for off in range(0, L, 512):
                        ln = min(512, L - off)
                        pp = pidx % 2
                        p.rec("tensor", lambda e, base=base, off=off, ln=ln,
                              pp=pp, t=t, qp=qp: e.matmul(
                                  PM[pp][:, 0:ln],
                                  qb[qp][:, t * 128:(t + 1) * 128],
                                  kbuf[:, base + off: base + off + ln],
                                  start=True, stop=True),
                              reads=[qb[qp], kbuf], writes=[PM[pp]])
                        if off + ln == L:
                            p.rec("vector", lambda e, ln=ln, pp=pp, cp=cp:
                                  e.tensor_add(PM[pp][:, ln - 128:ln],
                                               PM[pp][:, ln - 128:ln],
                                               C["maskA"][:, cp * 128:(cp + 1) * 128]),
                                  reads=[C["maskA"]], writes=[PM[pp]])
                        p.rec("scalar", lambda e, base=base, off=off, ln=ln,
                              pp=pp: e.activation(
                                  pstrip[:, base + off: base + off + ln],
                                  PM[pp][:, 0:ln], AF.Exp, scale=SCALE,
                                  accum_out=lsl[:, 1 + pp:2 + pp]),
                              reads=[PM[pp]], writes=[lsl],
                              extra=(pstrip.rd + [pstrip.wr]) if pidx == 0
                              else ())
                        if pidx == 0:
                            p.rec("vector", lambda e, pp=pp: e.tensor_copy(
                                lsl[:, 0:1], lsl[:, 1 + pp:2 + pp]),
                                reads=[lsl], writes=[lsl])
                        else:
                            p.rec("vector", lambda e, pp=pp: e.tensor_add(
                                lsl[:, 0:1], lsl[:, 0:1], lsl[:, 1 + pp:2 + pp]),
                                reads=[lsl], writes=[lsl])
                        pidx += 1
                p.rec("vector", lambda e: e.reciprocal(lsl[:, 3:4], lsl[:, 0:1]),
                      reads=[lsl], writes=[lsl])
                if "dbg_p" in io and h == 0 and t == 1:
                    p.dma("scalar", io["dbg_p"], pstrip[:, :],
                          reads=[pstrip, lsl])
                    p.dma("scalar", io["dbg_l"], lsl[:, :], reads=[lsl])
                for cp in range(8):
                    p.rec("vector", lambda e, cp=cp, L=L: e.tensor_scalar_mul(
                        pstrip[:, cp * TPC: cp * TPC + L],
                        pstrip[:, cp * TPC: cp * TPC + L], lsl[:, 3:4]),
                        reads=[lsl], writes=[pstrip] if cp == 7 else [],
                        extra=([pstrip.wr] + pstrip.rd) if cp == 0 else ())
                po = PO[(h * NT + t) % 2]
                ntile = 8 * (t + 1)
                cols = [cp * TPC + u * 128
                        for cp in range(8) for u in range(t + 1)]
                for grp in range(ntile // 4):
                    par = grp % 2
                    for j in range(4):
                        i = grp * 4 + j
                        p.rec("tensor", lambda e, col=cols[i], j=j, par=par:
                              e.transpose(
                                  PT[par][:, j * 128:(j + 1) * 128],
                                  pstrip[:, col:col + 128], C["identb"][:, :]),
                              reads=[pstrip, C["identb"]],
                              writes=[PT[par]] if j == 3 else [],
                              extra=([PT[par].wr] + PT[par].rd) if j == 0
                              else ())
                    p.rec("vector", lambda e, par=par: e.tensor_copy(
                        ptb[par][:, :], PT[par][:, :]),
                        reads=[PT[par]], writes=[ptb[par]])
                    for j in range(4):
                        i = grp * 4 + j
                        p.rec("tensor", lambda e, col=cols[i], j=j, par=par,
                              po=po, first=(i == 0), last=(i == ntile - 1):
                              e.matmul(
                                  po[:, 0:128], vbuf[:, col:col + 128],
                                  ptb[par][:, j * 128:(j + 1) * 128],
                                  start=first, stop=last),
                              reads=[vbuf, ptb[par]],
                              writes=[po] if i == ntile - 1 else [],
                              extra=([po.wr] + po.rd) if i == 0 else ())
                p.rec("scalar", lambda e, h=h, t=t, po=po: e.activation(
                    att[:, h * TPC + t * 128: h * TPC + (t + 1) * 128],
                    po[:, 0:128], AF.Copy),
                    reads=[po], writes=[att], extra=[att.wr])


def emit_combine(p, al, C, PS, SC, cfg, io, att, xout_buf):
    wbuf = SC["w"]
    xa = SC["f"][0:2]
    t1 = SC["f"][2]
    ff32 = SC["f"][3]
    ffb = SC["b"][0]
    pm = PS["PM"][2:4]
    has_bias = "cbias" in io
    if has_bias:
        cb2 = io["cbias_sb"]
        p.dma("sync", cb2[:, 0:2 * NDC], io["cbias"], writes=[cb2])
    for oc in range(NDC):
        b = oc % 2
        p.dma("sync", wbuf[b][:, :],
              io["WoT"][:, oc * (NDC * 128):(oc + 1) * (NDC * 128)],
              writes=[wbuf[b]])
        p.dma("sync", xa[b][:, :], io["x"][oc * 128:(oc + 1) * 128, :],
              writes=[xa[b]], reads=[io["x_dep"]] if "x_dep" in io else ())
        p.dma("sync", ffb[:, :], io["ffn"][oc * 128:(oc + 1) * 128, :],
              writes=[ffb], reads=[io["ffn_buf"]])
        p.rec("vector", lambda e: e.tensor_copy(ff32[:, :], ffb[:, :]),
              reads=[ffb], writes=[ff32])
        for tb in range(2):
            for c in range(NDC):
                p.rec("tensor", lambda e, b=b, c=c, tb=tb: e.matmul(
                    pm[tb][:, :], wbuf[b][:, c * 128:(c + 1) * 128],
                    att[:, c * TPC + tb * 512: c * TPC + tb * 512 + 512],
                    start=(c == 0), stop=(c == NDC - 1)),
                    reads=[wbuf[b], att], writes=[pm[tb]] if c == NDC - 1 else [],
                    extra=([pm[tb].wr] + pm[tb].rd) if c == 0 else ())
        for tb in range(2):
            sl = slice(tb * 512, (tb + 1) * 512)
            if has_bias:
                p.rec("vector", lambda e, tb=tb, sl=sl, oc=oc: e.tensor_scalar(
                    t1[:, sl], pm[tb][:, :], cb2[:, oc:oc + 1],
                    cb2[:, NDC + oc:NDC + oc + 1], ALU.add, ALU.add),
                    reads=[pm[tb], cb2], writes=[t1] if tb == 1 else [],
                    extra=(t1.rd + [t1.wr]) if tb == 0 else ())
            else:
                p.rec("vector", lambda e, tb=tb, sl=sl: e.tensor_copy(
                    t1[:, sl], pm[tb][:, :]),
                    reads=[pm[tb]], writes=[t1] if tb == 1 else [],
                    extra=(t1.rd + [t1.wr]) if tb == 0 else ())
        p.rec("vector", lambda e, b=b: e.tensor_add(t1[:, :], t1[:, :],
                                                    xa[b][:, :]),
              reads=[t1, xa[b]], writes=[t1])
        p.rec("vector", lambda e: e.tensor_add(t1[:, :], t1[:, :], ff32[:, :]),
              reads=[t1, ff32], writes=[t1])
        p.dma("scalar", xout_buf.ap[oc * 128:(oc + 1) * 128, :], t1[:, :],
              reads=[t1], writes=[xout_buf], extra=[xout_buf.wr])


def make_scratch(al, tag=""):
    return {
        "f": [al.sb(f"sc_f{i}{tag}", [128, TPC], F32) for i in range(4)],
        "b": [al.sb(f"sc_b{i}{tag}", [128, TPC], BF16) for i in range(2)],
        "w": [al.sb(f"sc_w{i}{tag}", [128, NDC * 128], BF16) for i in range(2)],
        "small": al.sb(f"sc_small{tag}", [128, 32], F32),
        "stat": al.sb(f"sc_stat{tag}", [1, 2 * TPC], F32),
        "nslab": al.sb(f"sc_nslab{tag}", [1, TPC], F32),
        "cmbT": al.sb(f"sc_cmbT{tag}", [8, TPC], BF16),
        "abc": al.sb(f"sc_abc{tag}", [128, TPC], F32),
        "bbc": al.sb(f"sc_bbc{tag}", [128, TPC], F32),
    }



def _dram(nc, name, shape, dt, kind):
    return nc.dram_tensor(name, list(shape), dt, kind=kind).ap()


def build_fused(core):
    nc = bass.Bass("TRN2", target_bir_lowering=False, debug=False)
    KV0, KV1 = L0["kv"], L1["kv"]
    RG = [list(range(NCORES))]

    io = {
        "x": _dram(nc, "x", (D, TPC), F32, "ExternalInput"),
        "WqT0": _dram(nc, "WqT0", (128, NH * NDC * 128), BF16, "ExternalInput"),
        "WkT0": _dram(nc, "WkT0", (128, KV0 * NDC * 128), BF16, "ExternalInput"),
        "WvT0": _dram(nc, "WvT0", (128, KV0 * NDC * 128), BF16, "ExternalInput"),
        "WoT0": _dram(nc, "WoT0", (128, NDC * NDC * 128), BF16, "ExternalInput"),
        "WrT": _dram(nc, "WrT", (128, NDC * 8), BF16, "ExternalInput"),
        "AT": _dram(nc, "AT", (128, NDC * 128), BF16, "ExternalInput"),
        "BT": _dram(nc, "BT", (128, D), BF16, "ExternalInput"),
        "WbT": _dram(nc, "WbT", (128, NDC * NDC * 128), BF16, "ExternalInput"),
        "bb": _dram(nc, "bb", (128, NDC), F32, "ExternalInput"),
        "cosT": _dram(nc, "cosT", (128, TPC), F32, "ExternalInput"),
        "sinS": _dram(nc, "sinS", (128, TPC), F32, "ExternalInput"),
        "WqT1": _dram(nc, "WqT1", (128, NH * NDC * 128), BF16, "ExternalInput"),
        "WkT1": _dram(nc, "WkT1", (128, KV1 * NDC * 128), BF16, "ExternalInput"),
        "WvT1": _dram(nc, "WvT1", (128, KV1 * NDC * 128), BF16, "ExternalInput"),
        "WoT1": _dram(nc, "WoT1", (128, NDC * NDC * 128), BF16, "ExternalInput"),
        "qkvb1": _dram(nc, "qkvb1", (128, 3 * NDC), F32, "ExternalInput"),
        "gateT": _dram(nc, "gateT", (128, NDC * 8), BF16, "ExternalInput"),
        "ics": _dram(nc, "ics", (8, D), BF16, "ExternalInput"),
        "ocs": _dram(nc, "ocs", (8, D), BF16, "ExternalInput"),
        "WsT": _dram(nc, "WsT", (128, NDC * NDC * 128), BF16, "ExternalInput"),
        "cbias": _dram(nc, "cbias", (128, 2 * NDC), F32, "ExternalInput"),
        "xo": _dram(nc, "xo", (D, TPC), F32, "ExternalOutput"),
    }
    for nm, arr in CONSTS.items():
        io[nm] = _dram(nc, "c_" + nm, arr.shape,
                       BF16 if arr.dtype == bf16 else F32, "ExternalInput")
    io["maskA"] = _dram(nc, "c_maskA", (128, 1024), F32, "ExternalInput")

    cc0i = nc.dram_tensor("cc0i", [128, 2 * KV0 * TPC], mybir.dt.bfloat16)
    cc0o = nc.dram_tensor("cc0o", [NCORES * 128, 2 * KV0 * TPC],
                          mybir.dt.bfloat16)
    cc1i = nc.dram_tensor("cc1i", [128, 2 * KV1 * TPC], mybir.dt.bfloat16)
    cc1o = nc.dram_tensor("cc1o", [NCORES * 128, 2 * KV1 * TPC],
                          mybir.dt.bfloat16)
    ikind = "ExternalOutput" if DEBUG else "Internal"
    qTd = _dram(nc, "qTd", (128, NH * TPC), BF16, ikind)
    ffnd = _dram(nc, "ffnd", (D, TPC), BF16, ikind)
    x1 = _dram(nc, "x1", (D, TPC), F32, ikind)
    if DEBUG:
        io["dbg_xn"] = _dram(nc, "dbg_xn", (128, NDC * TPC), BF16,
                             "ExternalOutput")
        io["dbg_xn1"] = _dram(nc, "dbg_xn1", (128, NDC * TPC), BF16,
                              "ExternalOutput")
        io["dbg_att"] = _dram(nc, "dbg_att", (128, NH * TPC), BF16,
                              "ExternalOutput")
        io["dbg_cc0o"] = _dram(nc, "dbg_cc0o", (NCORES * 128, 2 * 8 * TPC),
                               BF16, "ExternalOutput")
        io["dbg_p"] = _dram(nc, "dbg_p", (128, SEQ), BF16, "ExternalOutput")
        io["dbg_l"] = _dram(nc, "dbg_l", (128, 8), F32, "ExternalOutput")

    p = P(nc)
    al = Alloc(nc)
    PS = make_pool(al)
    SC = make_scratch(al)
    C = emit_consts(p, al, io)
    AT = make_attn_bufs(al)
    big = al.sb("big", [128, NDC * TPC], BF16)

    cosT_sb = al.sb("pj_cos", [128, TPC], F32)
    sinS_sb = al.sb("pj_sin", [128, TPC], F32)
    qkvb_sb = cosT_sb
    cbias_sb = sinS_sb

    qT_buf = Buf(qTd, "qTd")
    ffn_buf = Buf(ffnd, "ffnd")
    x1_buf = Buf(x1, "x1")
    xo_buf = Buf(io["xo"], "xo")
    cc0i_buf = Buf(cc0i.ap(), "cc0i")
    cc0o_buf = Buf(cc0o.ap(), "cc0o")
    cc1i_buf = Buf(cc1i.ap(), "cc1i")
    cc1o_buf = Buf(cc1o.ap(), "cc1o")

    io0 = dict(io)
    io0.update(WkT=io["WkT0"], WvT=io["WvT0"], WqT=io["WqT0"],
               ccin=cc0i.ap(), ccin_buf=cc0i_buf,
               cosT_sb=cosT_sb, sinS_sb=sinS_sb,
               qT=qTd, qT_buf=qT_buf, ffn_out=ffnd, ffn_buf=ffn_buf)
    emit_layernorm(p, al, C, PS, SC, io["x"], big)
    if DEBUG:
        p.dma("scalar", io["dbg_xn"], big[:, :], reads=[big])
    emit_projections(p, al, C, PS, SC, L0, io0, big)
    cc0_ch = p.chain("cc0_done", "gpsimd")
    p.rec("gpsimd", lambda e: e.collective_compute(
        "AllGather", ALU.bypass, replica_groups=RG,
        ins=[cc0i.ap().opt()], outs=[cc0o.ap().opt()]),
        reads=[cc0i_buf], writes=[cc0o_buf], inc_n=1, chain=cc0_ch)
    emit_q_heads(p, al, C, PS, SC, L0, io0, big)
    emit_ffn_moe(p, al, C, PS, SC, AT, io0, big)
    io0e = dict(ccr=cc0o.ap().rearrange("(c p) w -> p c w", p=128),
                cco_buf=cc0o_buf, qT=qTd, qT_buf=qT_buf,
                x=io["x"], ffn=ffnd, ffn_buf=ffn_buf, WoT=io["WoT0"])
    if DEBUG:
        io0e["dbg_p"] = io["dbg_p"]
        io0e["dbg_l"] = io["dbg_l"]
        p.dma("scalar", io["dbg_cc0o"], cc0o.ap()[:, :], reads=[cc0o_buf])
    emit_attention(p, al, C, PS, SC, AT, L0, io0e, big, core)
    if DEBUG:
        p.dma("scalar", io["dbg_att"], big[:, :], reads=[big])
    emit_combine(p, al, C, PS, SC, L0, io0e, big, x1_buf)

    io1 = dict(io)
    io1.update(WkT=io["WkT1"], WvT=io["WvT1"], WqT=io["WqT1"],
               ccin=cc1i.ap(), ccin_buf=cc1i_buf,
               qkvb=io["qkvb1"], qkvb_sb=qkvb_sb,
               qT=qTd, qT_buf=qT_buf, ffn_out=ffnd, ffn_buf=ffn_buf)
    emit_layernorm(p, al, C, PS, SC, x1, big, x_dep=x1_buf)
    if DEBUG:
        p.dma("scalar", io["dbg_xn1"], big[:, :], reads=[big])
    emit_projections(p, al, C, PS, SC, L1, io1, big)
    cc1_ch = p.chain("cc1_done", "gpsimd")
    p.rec("gpsimd", lambda e: e.collective_compute(
        "AllGather", ALU.bypass, replica_groups=RG,
        ins=[cc1i.ap().opt()], outs=[cc1o.ap().opt()]),
        reads=[cc1i_buf], writes=[cc1o_buf], inc_n=1, chain=cc1_ch)
    emit_q_heads(p, al, C, PS, SC, L1, io1, big)
    emit_ffn_chan(p, al, C, PS, SC, AT, io1, big)
    io1e = dict(ccr=cc1o.ap().rearrange("(c p) w -> p c w", p=128),
                cco_buf=cc1o_buf, qT=qTd, qT_buf=qT_buf,
                x=x1, x_dep=x1_buf, ffn=ffnd, ffn_buf=ffn_buf,
                WoT=io["WoT1"], cbias=io["cbias"], cbias_sb=cbias_sb)
    emit_attention(p, al, C, PS, SC, AT, L1, io1e, big, core)
    emit_combine(p, al, C, PS, SC, L1, io1e, big, xo_buf)

    p.barrier_all()
    p.finish()
    al.close()
    return nc



def make_weight_map(inputs):
    asc = np.ascontiguousarray

    def b(a):
        return np.ascontiguousarray(a.astype(bf16))

    return {
        "WqT0": b(wlay(asc(np.asarray(inputs["L0_attn_Wq"]).T))),
        "WkT0": b(wlay(asc(np.asarray(inputs["L0_attn_Wk"]).T))),
        "WvT0": b(wlay(asc(np.asarray(inputs["L0_attn_Wv"]).T))),
        "WoT0": b(wlay(asc(np.asarray(inputs["L0_attn_Wo"]).T))),
        "WrT": b(wlay(asc(np.asarray(inputs["L0_ffn_W_router"]).T), o_tile=8)),
        "AT": b(wlay(asc(
            np.asarray(inputs["L0_ffn_A"]).transpose(2, 0, 1).reshape(D, 128)))),
        "BT": b(asc(
            np.asarray(inputs["L0_ffn_B"]).transpose(0, 2, 1).reshape(128, D))),
        "WbT": b(wlay(asc(np.asarray(inputs["L0_ffn_W_base"]).T))),
        "bb": asc(np.asarray(inputs["L0_ffn_b_base"]).reshape(NDC, 128).T),
        "WqT1": b(wlay(asc(np.asarray(inputs["L1_attn_Wq"]).T))),
        "WkT1": b(wlay(asc(np.asarray(inputs["L1_attn_Wk"]).T))),
        "WvT1": b(wlay(asc(np.asarray(inputs["L1_attn_Wv"]).T))),
        "WoT1": b(wlay(asc(np.asarray(inputs["L1_attn_Wo"]).T))),
        "qkvb1": asc(np.concatenate(
            [np.asarray(inputs["L1_attn_bq"]).reshape(NDC, 128).T,
             np.asarray(inputs["L1_attn_bk"]).reshape(NDC, 128).T,
             np.asarray(inputs["L1_attn_bv"]).reshape(NDC, 128).T],
            axis=1)),
        "gateT": b(wlay(asc(np.asarray(inputs["L1_ffn_gate_w"]).T), o_tile=8)),
        "ics": b(asc(np.asarray(inputs["L1_ffn_in_channel_scale"]))),
        "ocs": b(asc(np.asarray(inputs["L1_ffn_out_channel_scale"]))),
        "WsT": b(wlay(asc(np.asarray(inputs["L1_ffn_weight"]).T))),
        "cbias": asc(np.concatenate(
            [np.asarray(inputs["L1_attn_bo"]).reshape(NDC, 128).T,
             np.asarray(inputs["L1_ffn_bias"]).reshape(NDC, 128).T], axis=1)),
    }


def kernel(**inputs):
    x2d = np.asarray(inputs["L0_attn_hidden_states"]).reshape(SEQ, D)
    wm = make_weight_map(inputs)
    cm = {("c_" + nm): arr for nm, arr in CONSTS.items()}
    maps = []
    for c in range(NCORES):
        cosT, sinS = rope_tables(c)
        d = dict(wm)
        d.update(cm)
        d["x"] = fm_shard(x2d, c)
        d["cosT"] = cosT
        d["sinS"] = sinS
        d["c_maskA"] = mask_add(c)
        maps.append(d)
    nc = build_fused(0)
    r = run_bass_kernel_spmd(nc, maps, core_ids=list(range(NCORES))).results
    out = unshard_fm([np.asarray(r[c]["xo"], np.float32)
                      for c in range(NCORES)])
    return out.reshape(1, SEQ, D)
